# version 03 (final)
# baseline (speedup 1.0000x reference)
import numpy as np
import ml_dtypes
from contextlib import ExitStack

import concourse.bass as bass
import concourse.mybir as mybir
from concourse.bass_utils import run_bass_kernel_spmd

F32 = mybir.dt.float32
BF16 = mybir.dt.bfloat16
AF = mybir.ActivationFunctionType
ALU = mybir.AluOpType
AX = mybir.AxisListType

D = 1024
T = 2048
NKT = 8
DI = 2730
DIP = 2816
NJ = 22
LN_EPS = 1e-5
QK_SCALE = 8.0
NEG = -1e30


class Prog:

    ENGS = ("tensor", "vector", "scalar", "gpsimd", "sync")

    def __init__(self, nc, stack):
        self.nc = nc
        self.stack = stack
        self.ops = {e: [] for e in self.ENGS}
        self.sems = {}
        self.cnt = {}
        self.last_wait = {}

    def sem(self, name):
        self.sems[name] = self.stack.enter_context(self.nc.semaphore(name))
        self.cnt[name] = 0
        return name

    def emit(self, eng, fn, waits=(), inc=None, add=1):
        wl = []
        for s, v in waits:
            if v is None or s is None:
                continue
            key = (eng, s)
            if self.last_wait.get(key, 0) >= v:
                continue
            self.last_wait[key] = v
            wl.append((s, v))
        tok = None
        if inc is not None:
            self.cnt[inc] += add
            tok = self.cnt[inc]
        self.ops[eng].append((fn, wl, inc, add))
        return tok

    def run(self, block):
        sems = self.sems

        def section(eng_name):
            def body(eng):
                for fn, wl, inc, add in self.ops[eng_name]:
                    for s, v in wl:
                        eng.wait_ge(sems[s], v)
                    inst = fn(eng)
                    if inc is not None:
                        inst.then_inc(sems[inc], add)
            return body

        block.tensor(section("tensor"))
        block.vector(section("vector"))
        block.scalar(section("scalar"))
        block.gpsimd(section("gpsimd"))
        block.sync(section("sync"))


class Banks:

    def __init__(self, nc, stack, n=8, width=512):
        self.t = [stack.enter_context(nc.psum_tensor(f"ps{i}", [128, width], F32))
                  for i in range(n)]
        self.writer = [None] * n
        self.readers = [[] for _ in range(n)]

    def acquire(self, i):
        return self.t[i], list(self.readers[i])

    def wrote(self, i, sem, tok):
        self.writer[i] = (sem, tok)
        self.readers[i] = []

    def reader_waits(self, i):
        return [self.writer[i]] if self.writer[i] else []

    def read(self, i, sem, tok):
        self.readers[i].append((sem, tok))


def build_ffn(with_final_ln: bool):
    nc = bass.Bass("TRN2", target_bir_lowering=False, debug=False)
    xin = nc.dram_tensor("xin", [D, T + 1], F32, kind="ExternalInput").ap()
    if with_final_ln:
        xslot = nc.dram_tensor("xslot", [D, T], F32, kind="ExternalInput").ap()
    w1h = nc.dram_tensor("w1h", [2 * NJ, 128, D], F32, kind="ExternalInput").ap()
    w2h = nc.dram_tensor("w2h", [NKT, 128, DIP], F32, kind="ExternalInput").ap()
    b1c = nc.dram_tensor("b1c", [128, 2 * NJ], F32, kind="ExternalInput").ap()
    b2c = nc.dram_tensor("b2c", [128, NKT], F32, kind="ExternalInput").ap()
    gc = nc.dram_tensor("gc", [128, NKT], F32, kind="ExternalInput").ap()
    blnc = nc.dram_tensor("blnc", [128, NKT], F32, kind="ExternalInput").ap()
    xout = nc.dram_tensor("xout", [D, T], F32, kind="ExternalOutput").ap()

    NB = T // 512

    stack = ExitStack()
    with stack:
        E = stack.enter_context
        xblk = [E(nc.sbuf_tensor(f"xblk{i}", [128, NKT * 513], F32)) for i in range(2)]
        xnb = E(nc.sbuf_tensor("xnb", [128, NKT * T], BF16))
        hg = E(nc.sbuf_tensor("hg", [128, NJ * T], BF16))
        w1b = [E(nc.sbuf_tensor(f"w1b{i}", [128, 2 * D], BF16)) for i in range(2)]
        w2b = [E(nc.sbuf_tensor(f"w2b{i}", [128, DIP], BF16)) for i in range(2)]
        x2 = [E(nc.sbuf_tensor(f"x2_{i}", [128, 512], F32)) for i in range(2)]
        gel = [E(nc.sbuf_tensor(f"gel{i}", [128, 512], BF16)) for i in range(2)]
        stage = [E(nc.sbuf_tensor(f"stage{i}", [128, 512], F32)) for i in range(4)]
        t1s = E(nc.sbuf_tensor("t1s", [128, 512], F32))
        t2s = E(nc.sbuf_tensor("t2s", [128, 512], F32))
        rows = E(nc.sbuf_tensor("rows", [1, 6 * 512], F32))
        b1s = E(nc.sbuf_tensor("b1s", [128, 2 * NJ], F32))
        b2s = E(nc.sbuf_tensor("b2s", [128, NKT], F32))
        gs = E(nc.sbuf_tensor("gs", [128, NKT], F32))
        blns = E(nc.sbuf_tensor("blns", [128, NKT], F32))
        onesc = E(nc.sbuf_tensor("onesc", [128, 1], F32))
        ones1 = E(nc.sbuf_tensor("ones1", [1, 128], F32))
        cst = E(nc.sbuf_tensor("cst", [128, 1], F32))
        nc.const_aps.register(F32, LN_EPS, cst[:, 0:1])

        p = Prog(nc, stack)
        for s in ("sW", "sB", "sT", "sV", "sS", "sG",
                  "sXa", "sXb", "sW1a", "sW1b", "sW2a", "sW2b",
                  "sG0", "sG1", "sG2", "sG3", "sF0", "sF1", "sF2", "sF3"):
            p.sem(s)
        SX = ("sXa", "sXb")
        SW1 = ("sW1a", "sW1b")
        SW2 = ("sW2a", "sW2b")
        SST = ("sG0", "sG1", "sG2", "sG3")
        SF = ("sF0", "sF1", "sF2", "sF3")
        bk = Banks(nc, stack)

        p.emit("gpsimd", lambda g: g.memset(onesc[:], 1.0), inc="sG")
        p.emit("gpsimd", lambda g: g.memset(ones1[:], 1.0), inc="sG")
        p.emit("gpsimd", lambda g: g.memset(cst[:], LN_EPS), inc="sG")
        for src, dst in ((b1c, b1s), (b2c, b2s), (gc, gs), (blnc, blns)):
            p.emit("gpsimd", lambda g, s=src, d=dst: g.dma_start(out=d[:], in_=s),
                   inc="sW", add=16)
        smalls = p.cnt["sW"]

        xtok = [None] * NB

        def load_xblk(i, waits=()):
            buf = xblk[i % 2]
            for kt in range(NKT):
                tok = p.emit(
                    "sync",
                    lambda e, kt=kt, i=i, buf=buf: e.dma_start(
                        out=buf[:, kt * 513:(kt + 1) * 513],
                        in_=xin[kt * 128:(kt + 1) * 128, i * 512:i * 512 + 513]),
                    waits=(waits if kt == 0 else ()), inc=SX[i % 2], add=16)
            xtok[i] = (SX[i % 2], tok)

        load_xblk(0)
        load_xblk(1)

        base_tok = p.emit("gpsimd",
                          lambda g: g.dma_start(out=xout, in_=xin[:, 1:T + 1]),
                          inc="sB", add=16)

        def xs_slice(buf, kt, off=0, width=512):
            s = 1 if kt < 4 else 0
            return buf[:, kt * 513 + s + off: kt * 513 + s + off + width]

        xnb_tok = None
        for i in range(NB):
            buf = xblk[i % 2]
            bs, bs_w = bk.acquire(0)
            bq, bq_w = bk.acquire(1)
            for kt in range(NKT):
                tok = p.emit("tensor",
                             lambda t, kt=kt, buf=buf, bs=bs: t.matmul(
                                 bs[0:1, :], onesc[:], xs_slice(buf, kt),
                                 start=(kt == 0), stop=(kt == NKT - 1)),
                             waits=[xtok[i], ("sG", 1)] + (bs_w if kt == 0 else []),
                             inc="sT")
            bk.wrote(0, "sT", tok)
            sq_toks = []
            for kt in range(NKT):
                x2b = x2[kt % 2]
                stok = p.emit("scalar",
                              lambda s, kt=kt, buf=buf, x2b=x2b: s.activation(
                                  x2b[:], xs_slice(buf, kt), AF.Square),
                              waits=[xtok[i], ("sT", sq_toks[-2][1] if kt >= 2 else None)],
                              inc="sS")
                mtok = p.emit("tensor",
                              lambda t, kt=kt, x2b=x2b, bq=bq: t.matmul(
                                  bq[0:1, :], onesc[:], x2b[:],
                                  start=(kt == 0), stop=(kt == NKT - 1)),
                              waits=[("sS", stok)] + (bq_w if kt == 0 else []),
                              inc="sT")
                sq_toks.append((stok, mtok))
            bk.wrote(1, "sT", mtok)
            R = lambda j: rows[0:1, j * 512:(j + 1) * 512]
            t_m = p.emit("vector", lambda v, bs=bs: v.tensor_single_scalar(
                R(0), bs[0:1, :], 1.0 / D, ALU.mult),
                waits=bk.reader_waits(0) + [("sT", p.cnt["sT"])], inc="sV")
            bk.read(0, "sV", t_m)
            t_v1 = p.emit("vector", lambda v, bq=bq: v.tensor_single_scalar(
                R(1), bq[0:1, :], 1.0 / D, ALU.mult), inc="sV")
            bk.read(1, "sV", t_v1)
            t_m2 = p.emit("vector", lambda v: v.tensor_mul(R(2), R(0), R(0)),
                          waits=[("sV", t_m)], inc="sV")
            t_var = p.emit("vector", lambda v: v.tensor_sub(R(3), R(1), R(2)),
                           waits=[("sV", t_m2)], inc="sV")
            t_ln = p.emit("scalar", lambda s: s.activation(R(4), R(3), AF.Ln, bias=LN_EPS),
                          waits=[("sV", t_var), ("sG", 3)], inc="sS")
            t_rstd = p.emit("scalar", lambda s: s.activation(R(4), R(4), AF.Exp, scale=-0.5),
                            waits=[("sS", t_ln)], inc="sS")
            t_q = p.emit("vector", lambda v: v.scalar_tensor_tensor(
                R(5), R(0), -1.0, R(4), ALU.mult, ALU.mult),
                waits=[("sS", t_rstd)], inc="sV")
            bP, bP_w = bk.acquire(2)
            bQ, bQ_w = bk.acquire(3)
            t_bP = p.emit("tensor", lambda t, bP=bP: t.matmul(
                bP[:], ones1[:], R(4), start=True, stop=True),
                waits=[("sS", t_rstd), ("sV", t_q)] + bP_w, inc="sT")
            bk.wrote(2, "sT", t_bP)
            t_bQ = p.emit("tensor", lambda t, bQ=bQ: t.matmul(
                bQ[:], ones1[:], R(5), start=True, stop=True),
                waits=[("sV", t_q)] + bQ_w, inc="sT")
            bk.wrote(3, "sT", t_bQ)
            prev_tt2 = None
            prev_xnb = None
            for kt in range(NKT):
                tt1 = p.emit("vector", lambda v, kt=kt, buf=buf, bP=bP: v.tensor_mul(
                    t1s[:], xs_slice(buf, kt), bP[:]),
                    waits=[("sT", t_bQ), ("sV", prev_tt2)], inc="sV")
                tt2 = p.emit("vector", lambda v, bQ=bQ: v.tensor_add(t2s[:], t1s[:], bQ[:]),
                             waits=[("sV", tt1), ("sV", prev_xnb)], inc="sV")
                prev_tt2 = tt2
                xnb_tok = p.emit("vector", lambda v, kt=kt, i=i: v.tensor_scalar(
                    xnb[:, kt * T + i * 512: kt * T + i * 512 + 512], t2s[:],
                    gs[:, kt:kt + 1], blns[:, kt:kt + 1], ALU.mult, ALU.add),
                    waits=[("sV", tt2), ("sW", smalls)], inc="sV")
                prev_xnb = xnb_tok
            bk.read(2, "sV", xnb_tok)
            bk.read(3, "sV", xnb_tok)
            if i + 2 < NB:
                load_xblk(i + 2, waits=[("sV", xnb_tok), ("sS", p.cnt["sS"]),
                                        ("sT", p.cnt["sT"])])
        ln_done = xnb_tok

        w1tok = [None] * NJ

        def load_w1(j):
            buf = w1b[j % 2]
            p.emit("gpsimd", lambda g, j=j, buf=buf: g.dma_start(
                out=buf[:, 0:D], in_=w1h[j]), inc=SW1[j % 2], add=16)
            w1tok[j] = p.emit("gpsimd", lambda g, j=j, buf=buf: g.dma_start(
                out=buf[:, D:2 * D], in_=w1h[NJ + j]), inc=SW1[j % 2], add=16)

        load_w1(0)
        load_w1(1)

        step = 0
        gel_last_read = [None, None]
        for j in range(NJ):
            for i in range(NB):
                ba_i, bg_i = 4 + 2 * (step % 2), 5 + 2 * (step % 2)
                ba, ba_w = bk.acquire(ba_i)
                bg, bg_w = bk.acquire(bg_i)
                for kt in range(NKT):
                    mt = p.emit("tensor", lambda t, kt=kt, j=j, i=i, ba=ba: t.matmul(
                        ba[:], w1b[j % 2][:, kt * 128:(kt + 1) * 128],
                        xnb[:, kt * T + i * 512: kt * T + (i + 1) * 512],
                        start=(kt == 0), stop=(kt == NKT - 1)),
                        waits=[(SW1[j % 2], w1tok[j]), ("sV", ln_done), ("sW", smalls)] + (ba_w if kt == 0 else []),
                        inc="sT")
                for kt in range(NKT):
                    mt = p.emit("tensor", lambda t, kt=kt, j=j, i=i, bg=bg: t.matmul(
                        bg[:], w1b[j % 2][:, D + kt * 128: D + (kt + 1) * 128],
                        xnb[:, kt * T + i * 512: kt * T + (i + 1) * 512],
                        start=(kt == 0), stop=(kt == NKT - 1)),
                        waits=(bg_w if kt == 0 else []), inc="sT")
                bk.wrote(ba_i, "sT", mt)
                bk.wrote(bg_i, "sT", mt)
                gb = gel[step % 2]
                gt = p.emit("scalar", lambda s, j=j, bg=bg, gb=gb: s.activation(
                    gb[:], bg[:], AF.Gelu, bias=b1s[:, NJ + j: NJ + j + 1]),
                    waits=bk.reader_waits(bg_i) + [("sT", mt), ("sW", smalls),
                                                   ("sV", gel_last_read[step % 2])],
                    inc="sS")
                bk.read(bg_i, "sS", gt)
                ht = p.emit("vector", lambda v, j=j, i=i, ba=ba, gb=gb: v.scalar_tensor_tensor(
                    hg[:, j * T + i * 512: j * T + (i + 1) * 512], ba[:],
                    b1s[:, j:j + 1], gb[:], ALU.add, ALU.mult),
                    waits=bk.reader_waits(ba_i) + [("sT", mt), ("sS", gt)], inc="sV")
                bk.read(ba_i, "sV", ht)
                gel_last_read[step % 2] = ht
                step += 1
            if j + 2 < NJ:
                jj = j + 2
                buf = w1b[jj % 2]
                p.emit("gpsimd", lambda g, jj=jj, buf=buf: g.dma_start(
                    out=buf[:, 0:D], in_=w1h[jj]),
                    waits=[("sT", p.cnt["sT"])], inc=SW1[jj % 2], add=16)
                w1tok[jj] = p.emit("gpsimd", lambda g, jj=jj, buf=buf: g.dma_start(
                    out=buf[:, D:2 * D], in_=w1h[NJ + jj]), inc=SW1[jj % 2], add=16)
        hg_done = ht

        w2tok = [None] * NKT

        def load_w2(dt):
            w2tok[dt] = p.emit("gpsimd", lambda g, dt=dt: g.dma_start(
                out=w2b[dt % 2][:], in_=w2h[dt]), inc=SW2[dt % 2], add=16)

        load_w2(0)
        load_w2(1)

        stage_last_read = [(None, None)] * 4
        step = 0
        for dt in range(NKT):
            for i in range(NB):
                bo_i = 4 + (step % 4)
                bo, bo_w = bk.acquire(bo_i)
                for j in range(NJ):
                    mt = p.emit("tensor", lambda t, j=j, dt=dt, i=i, bo=bo: t.matmul(
                        bo[:], w2b[dt % 2][:, j * 128:(j + 1) * 128],
                        hg[:, j * T + i * 512: j * T + (i + 1) * 512],
                        start=(j == 0), stop=(j == NJ - 1)),
                        waits=[(SW2[dt % 2], w2tok[dt]), ("sV", hg_done)] + (bo_w if j == 0 else []),
                        inc="sT")
                bk.wrote(bo_i, "sT", mt)
                sg = stage[step % 4]
                et = p.emit("vector", lambda v, dt=dt, bo=bo, sg=sg: v.tensor_single_scalar(
                    sg[:], bo[:], b2s[:, dt:dt + 1], ALU.add),
                    waits=bk.reader_waits(bo_i) + [("sT", mt),
                                                   stage_last_read[step % 4]],
                    inc="sV")
                bk.read(bo_i, "sV", et)
                ot = p.emit("gpsimd", lambda g, dt=dt, i=i, sg=sg: g.dma_start(
                    out=xout[dt * 128:(dt + 1) * 128, i * 512:(i + 1) * 512],
                    in_=sg[:], accum_op=ALU.add),
                    waits=[("sB", base_tok), ("sV", et)], inc=SST[step % 4], add=16)
                stage_last_read[step % 4] = (SST[step % 4], ot)
                step += 1
            if dt + 2 < NKT:
                ddt = dt + 2
                w2tok[ddt] = p.emit("gpsimd", lambda g, ddt=ddt: g.dma_start(
                    out=w2b[ddt % 2][:], in_=w2h[ddt]),
                    waits=[("sT", p.cnt["sT"])], inc=SW2[ddt % 2], add=16)

        final_toks = [("sB", base_tok)] + [(s, p.cnt[s]) for s in SST]

        if with_final_ln:
            xsl = hg.ap().bitcast(F32)[:, 0:NKT * 512]
            for i in range(NB):
                buf = xblk[i % 2]
                ld = None
                for kt in range(NKT):
                    ld = p.emit("sync", lambda e, kt=kt, i=i, buf=buf: e.dma_start(
                        out=buf[:, kt * 513:kt * 513 + 512],
                        in_=xout[kt * 128:(kt + 1) * 128, i * 512:(i + 1) * 512]),
                        waits=final_toks + [("sV", p.cnt["sV"])], inc=SX[i % 2], add=16)
                ld = (SX[i % 2], ld)
                lds = None
                for kt in range(NKT):
                    lds = p.emit("sync", lambda e, kt=kt, i=i: e.dma_start(
                        out=xsl[:, kt * 512:(kt + 1) * 512],
                        in_=xslot[kt * 128:(kt + 1) * 128, i * 512:(i + 1) * 512]),
                        waits=[("sV", p.cnt["sV"]), ("sT", p.cnt["sT"])],
                        inc=SX[i % 2], add=16)
                lds = (SX[i % 2], lds)
                addt = None
                for kt in range(NKT):
                    addt = p.emit("vector", lambda v, kt=kt, buf=buf: v.tensor_add(
                        buf[:, kt * 513:kt * 513 + 512],
                        buf[:, kt * 513:kt * 513 + 512],
                        xsl[:, kt * 512:(kt + 1) * 512]),
                        waits=[lds], inc="sV")
                ld = ("sV", addt)
                bs, bs_w = bk.acquire(0)
                bq, bq_w = bk.acquire(1)
                for kt in range(NKT):
                    tok = p.emit("tensor", lambda t, kt=kt, buf=buf, bs=bs: t.matmul(
                        bs[0:1, :], onesc[:], buf[:, kt * 513:kt * 513 + 512],
                        start=(kt == 0), stop=(kt == NKT - 1)),
                        waits=[ld] + (bs_w if kt == 0 else []), inc="sT")
                bk.wrote(0, "sT", tok)
                sq_toks = []
                for kt in range(NKT):
                    x2b = x2[kt % 2]
                    stok = p.emit("scalar", lambda s, kt=kt, buf=buf, x2b=x2b: s.activation(
                        x2b[:], buf[:, kt * 513:kt * 513 + 512], AF.Square),
                        waits=[ld, ("sT", sq_toks[-2][1] if kt >= 2 else None)],
                        inc="sS")
                    mtok = p.emit("tensor", lambda t, x2b=x2b, kt=kt, bq=bq: t.matmul(
                        bq[0:1, :], onesc[:], x2b[:],
                        start=(kt == 0), stop=(kt == NKT - 1)),
                        waits=[("sS", stok)] + (bq_w if kt == 0 else []), inc="sT")
                    sq_toks.append((stok, mtok))
                bk.wrote(1, "sT", mtok)
                R = lambda j: rows[0:1, j * 512:(j + 1) * 512]
                t_m = p.emit("vector", lambda v, bs=bs: v.tensor_single_scalar(
                    R(0), bs[0:1, :], 1.0 / D, ALU.mult),
                    waits=[("sT", p.cnt["sT"])], inc="sV")
                bk.read(0, "sV", t_m)
                t_v1 = p.emit("vector", lambda v, bq=bq: v.tensor_single_scalar(
                    R(1), bq[0:1, :], 1.0 / D, ALU.mult), inc="sV")
                bk.read(1, "sV", t_v1)
                t_m2 = p.emit("vector", lambda v: v.tensor_mul(R(2), R(0), R(0)),
                              waits=[("sV", t_m)], inc="sV")
                t_var = p.emit("vector", lambda v: v.tensor_sub(R(3), R(1), R(2)),
                               waits=[("sV", t_m2)], inc="sV")
                t_ln = p.emit("scalar", lambda s: s.activation(R(4), R(3), AF.Ln, bias=LN_EPS),
                              waits=[("sV", t_var), ("sG", 3)], inc="sS")
                t_rstd = p.emit("scalar", lambda s: s.activation(R(4), R(4), AF.Exp, scale=-0.5),
                                waits=[("sS", t_ln)], inc="sS")
                t_q = p.emit("vector", lambda v: v.scalar_tensor_tensor(
                    R(5), R(0), -1.0, R(4), ALU.mult, ALU.mult),
                    waits=[("sS", t_rstd)], inc="sV")
                bP, bP_w = bk.acquire(2)
                bQ, bQ_w = bk.acquire(3)
                t_bP = p.emit("tensor", lambda t, bP=bP: t.matmul(
                    bP[:], ones1[:], R(4), start=True, stop=True),
                    waits=[("sS", t_rstd), ("sV", t_q)] + bP_w, inc="sT")
                bk.wrote(2, "sT", t_bP)
                t_bQ = p.emit("tensor", lambda t, bQ=bQ: t.matmul(
                    bQ[:], ones1[:], R(5), start=True, stop=True),
                    waits=[("sV", t_q)] + bQ_w, inc="sT")
                bk.wrote(3, "sT", t_bQ)
                last_v = None
                prev_tt2 = None
                for kt in range(NKT):
                    tt1 = p.emit("vector", lambda v, kt=kt, buf=buf, bP=bP: v.tensor_mul(
                        t1s[:], buf[:, kt * 513:kt * 513 + 512], bP[:]),
                        waits=[("sT", t_bQ), ("sV", prev_tt2)], inc="sV")
                    sgb = stage[kt % 4]
                    tt2 = p.emit("vector", lambda v, bQ=bQ, sgb=sgb: v.tensor_add(
                        sgb[:], t1s[:], bQ[:]),
                        waits=[("sV", tt1), stage_last_read[kt % 4]], inc="sV")
                    last_v = tt2
                    prev_tt2 = tt2
                    ot = p.emit("sync", lambda e, kt=kt, i=i, sgb=sgb: e.dma_start(
                        out=xout[kt * 128:(kt + 1) * 128, i * 512:(i + 1) * 512],
                        in_=sgb[:]),
                        waits=[("sV", tt2)], inc=SF[kt % 4], add=16)
                    stage_last_read[kt % 4] = (SF[kt % 4], ot)
                bk.read(2, "sV", last_v)
                bk.read(3, "sV", last_v)
            final_toks = [(s, p.cnt[s]) for s in SF] + [("sB", base_tok)]

        p.emit("sync", lambda e: e.nop(), waits=final_toks)

        with nc.Block() as block:
            p.run(block)

    return nc


HALO = 258
HK = 1280
HQ = 1024


def build_attn():
    nc = bass.Bass("TRN2", target_bir_lowering=False, debug=False)
    TT_ = T + HALO
    xin = nc.dram_tensor("xin", [D, TT_], F32, kind="ExternalInput").ap()
    wqkh = nc.dram_tensor("wqkh", [16, 128, D], F32, kind="ExternalInput").ap()
    wvh = nc.dram_tensor("wvh", [NKT, 128, D], F32, kind="ExternalInput").ap()
    woh = nc.dram_tensor("woh", [NKT, 128, D], F32, kind="ExternalInput").ap()
    qsc = nc.dram_tensor("qsc", [128, 1], F32, kind="ExternalInput").ap()
    ksc = nc.dram_tensor("ksc", [128, 1], F32, kind="ExternalInput").ap()
    tabs_d = {n: nc.dram_tensor(n, [128, TT_], F32, kind="ExternalInput").ap()
              for n in ("tqc", "tqs", "tkc", "tks")}
    maskA_d = {n: nc.dram_tensor(n, [128, 384], F32, kind="ExternalInput").ap()
               for n in ("maskA0", "maskA1", "maskAr")}
    maskB_d = {n: nc.dram_tensor(n, [128, 128], F32, kind="ExternalInput").ap()
               for n in ("maskB00", "maskB01", "maskB02",
                         "maskB10", "maskB11", "maskB12", "maskBr0", "maskBr2")}
    idm = nc.dram_tensor("idm", [128, 128], F32, kind="ExternalInput").ap()
    selm = nc.dram_tensor("selm", [16, 16 * 128], F32, kind="ExternalInput").ap()
    xout = nc.dram_tensor("xout", [D, T], F32, kind="ExternalOutput").ap()

    stack = ExitStack()
    with stack:
        E = stack.enter_context
        xb = E(nc.sbuf_tensor("xb", [128, NKT * TT_], BF16))
        qt = E(nc.sbuf_tensor("qt", [128, 8 * HQ], BF16))
        kt_ = E(nc.sbuf_tensor("kt_", [128, 8 * HK], BF16))
        vt = E(nc.sbuf_tensor("vt", [128, 10 * 1040], BF16))
        mix = E(nc.sbuf_tensor("mix", [128, 8 * HQ], BF16))
        tabs = {n: E(nc.sbuf_tensor("t_" + n, [128, 1282], F32))
                for n in ("tqc", "tqs", "tkc", "tks")}
        wqk = [E(nc.sbuf_tensor(f"wqk{i}", [128, D], BF16)) for i in range(2)]
        wv = E(nc.sbuf_tensor("wv", [128, NKT * D], BF16))
        wo = E(nc.sbuf_tensor("wo", [128, NKT * D], BF16))
        maskA = {n: E(nc.sbuf_tensor("m_" + n, [128, 384], F32)) for n in maskA_d}
        maskB = {n: E(nc.sbuf_tensor("m_" + n, [128, 128], F32)) for n in maskB_d}
        idms = E(nc.sbuf_tensor("idms", [128, 128], F32))
        selms = E(nc.sbuf_tensor("selms", [16, 16 * 128], BF16))
        qscs = E(nc.sbuf_tensor("qscs", [128, 1], F32))
        kscs = E(nc.sbuf_tensor("kscs", [128, 1], F32))
        bdone = E(nc.sbuf_tensor("bdone", [128, 128], F32))
        ones1 = E(nc.sbuf_tensor("ones1", [1, 64], F32))
        q2 = [E(nc.sbuf_tensor(f"q2_{i}", [128, 512], F32)) for i in range(2)]
        rsc = [E(nc.sbuf_tensor(f"rsc{i}", [128, 512], F32)) for i in range(2)]
        qp = [E(nc.sbuf_tensor(f"qp{i}", [128, 512], F32)) for i in range(2)]
        qr = rsc
        t1q = q2
        maxb = E(nc.sbuf_tensor("maxb", [128, 16], F32))
        ngm = E(nc.sbuf_tensor("ngm", [16, 128], BF16))
        att = [E(nc.sbuf_tensor(f"att{i}", [128, 384], BF16)) for i in range(2)]
        avs = [E(nc.sbuf_tensor(f"avs{i}", [64, 128], F32)) for i in range(2)]
        rs1 = E(nc.sbuf_tensor("rs1", [1, 128], F32))
        stage = [E(nc.sbuf_tensor(f"stage{i}", [128, 512], F32)) for i in range(4)]

        p = Prog(nc, stack)
        for s in ("sW", "sB", "sTab", "sT", "sV", "sS", "sG",
                  "sQa", "sQb", "sG0", "sG1", "sG2", "sG3"):
            p.sem(s)
        SQ = ("sQa", "sQb")
        SST = ("sG0", "sG1", "sG2", "sG3")
        bk = Banks(nc, stack)

        p.emit("gpsimd", lambda g: g.memset(ones1[:], 1.0), inc="sG")
        gz = p.emit("gpsimd", lambda g: g.memset(bdone[:], 0.0), inc="sG")
        gz = p.emit("gpsimd", lambda g: g.memset(bdone[0:64, 0:64], 1.0),
                    waits=[("sG", gz)], inc="sG")
        p.emit("gpsimd", lambda g: g.memset(bdone[64:128, 64:128], 1.0),
               waits=[("sG", gz)], inc="sG")
        vones = vt.ap().rearrange("p (i g c) -> p i g c", i=10, c=65)[:, :, :, 64:65]
        p.emit("gpsimd", lambda g: g.memset(vones, 1.0), inc="sG")
        g_init = p.cnt["sG"]
        for src, dst in [(qsc, qscs), (ksc, kscs), (idm, idms), (selm, selms)] + \
                        [(maskA_d[n], maskA[n]) for n in maskA_d] + \
                        [(maskB_d[n], maskB[n]) for n in maskB_d]:
            p.emit("gpsimd", lambda g, s=src, d=dst: g.dma_start(out=d[:], in_=s),
                   inc="sW", add=16)
        for ktl in range(NKT):
            p.emit("gpsimd", lambda g, ktl=ktl: g.dma_start(
                out=xb[:, ktl * TT_:(ktl + 1) * TT_],
                in_=xin[ktl * 128:(ktl + 1) * 128, :]), inc="sW", add=16)
        for ktl in range(NKT):
            p.emit("gpsimd", lambda g, ktl=ktl: g.dma_start(
                out=wv[:, ktl * D:(ktl + 1) * D], in_=wvh[ktl]), inc="sW", add=16)
        for dt in range(NKT):
            p.emit("gpsimd", lambda g, dt=dt: g.dma_start(
                out=wo[:, dt * D:(dt + 1) * D], in_=woh[dt]), inc="sW", add=16)
        smalls = p.cnt["sW"]

        base_tok = p.emit("gpsimd",
                          lambda g: g.dma_start(out=xout, in_=xin[:, HALO:]),
                          inc="sB", add=16)

        pipe_prev_fin = [None, None]

        def qk_pipeline(ui, P0, P0i, P1i, tabc, tabn, scl, u0, w, dst, dstoff, mm_tok,
                        tab_tok, guard):
            P1, P1_w = bk.acquire(P1i)
            sq = p.emit("scalar", lambda s: s.activation(q2[ui][:, 0:w], P0[:, 0:w], AF.Square),
                        waits=bk.reader_waits(P0i) + [("sT", mm_tok),
                                                      ("sV", pipe_prev_fin[ui])], inc="sS")
            bd = p.emit("tensor", lambda t: t.matmul(P1[:, 0:w], bdone[:], q2[ui][:, 0:w],
                                                     start=True, stop=True),
                        waits=[("sS", sq), ("sG", g_init)] + P1_w, inc="sT")
            bk.wrote(P1i, "sT", bd)
            mx = p.emit("vector", lambda v: v.tensor_scalar_max(rsc[ui][:, 0:w], P1[:, 0:w], 1e-24),
                        waits=bk.reader_waits(P1i) + [("sT", bd),
                                                      ("sV", pipe_prev_fin[ui])], inc="sV")
            bk.read(P1i, "sV", mx)
            ln = p.emit("scalar", lambda s: s.activation(rsc[ui][:, 0:w], rsc[ui][:, 0:w],
                                                         AF.Ln), waits=[("sV", mx)], inc="sS")
            ex = p.emit("scalar", lambda s: s.activation(rsc[ui][:, 0:w], rsc[ui][:, 0:w],
                                                         AF.Exp, scale=-0.5),
                        waits=[("sS", ln)], inc="sS")
            qpt = p.emit("vector", lambda v: v.scalar_tensor_tensor(
                qp[ui][:, 0:w], P0[:, 0:w], scl, rsc[ui][:, 0:w], ALU.mult, ALU.mult),
                waits=[("sS", ex)], inc="sV")
            bk.read(P0i, "sV", qpt)
            rt = None
            for (a, b) in ((0, 32), (32, 0), (64, 96), (96, 64)):
                rt = p.emit("scalar", lambda s, a=a, b=b: s.copy(
                    qr[ui][a:a + 32, 0:w], qp[ui][b:b + 32, 0:w]),
                    waits=[("sV", qpt)], inc="sS")
            tc = p.emit("vector", lambda v: v.tensor_mul(
                t1q[ui][:, 0:w], qp[ui][:, 0:w], tabs[tabc][:, u0:u0 + w]),
                waits=[("sV", qpt), ("sTab", tab_tok)], inc="sV")
            ts_ = p.emit("vector", lambda v: v.tensor_mul(
                qr[ui][:, 0:w], qr[ui][:, 0:w], tabs[tabn][:, u0:u0 + w]),
                waits=[("sS", rt)], inc="sV")
            fin = p.emit("vector", lambda v: v.tensor_add(
                dst[:, dstoff:dstoff + w], t1q[ui][:, 0:w], qr[ui][:, 0:w]),
                waits=[("sV", ts_), ("sV", tc), ("sT", guard)], inc="sV")
            pipe_prev_fin[ui] = fin
            return fin

        st_counter = [0]
        stage_last_read = [None] * 4
        att_free = [None, None]
        avs_free = [None, None]
        rs1_free = [None]
        half_guard = None
        for h in range(2):
            tab_tok = None
            for n in ("tqc", "tqs", "tkc", "tks"):
                tab_tok = p.emit("sync", lambda e, n=n, h=h: e.dma_start(
                    out=tabs[n][:], in_=tabs_d[n][:, h * 1024:h * 1024 + 1282]),
                    waits=[("sV", p.cnt["sV"])], inc="sTab", add=16)
            wqk_tok = [None] * 16
            wqk_free = [half_guard, half_guard]

            def load_wqk(ct, wqk_tok=wqk_tok, wqk_free=wqk_free):
                wqk_tok[ct] = p.emit("gpsimd", lambda g, ct=ct: g.dma_start(
                    out=wqk[ct % 2][:], in_=wqkh[ct]),
                    waits=[("sT", wqk_free[ct % 2])], inc=SQ[ct % 2], add=16)

            load_wqk(0)
            load_wqk(1)
            ui = 0
            pipe_done = []
            for ct in range(16):
                isq = ct < 8
                nblk = 2 if isq else 3
                for b in range(nblk):
                    w = 512 if (isq or b < 2) else 256
                    if isq:
                        u = HALO + h * 1024 + b * 512
                        dst, dstoff = qt, ct * HQ + b * 512
                        tabc, tabn, scl = "tqc", "tqs", qscs[:, 0:1]
                        tu = 258 + b * 512
                    else:
                        u = 2 + h * 1024 + b * 512
                        dst, dstoff = kt_, (ct - 8) * HK + b * 512
                        tabc, tabn, scl = "tkc", "tks", kscs[:, 0:1]
                        tu = 2 + b * 512
                    P0i = ui % 2
                    P1i = 2 + ui % 2
                    P0, P0_w = bk.acquire(P0i)
                    for ktl in range(NKT):
                        us = u - (1 if ktl >= 4 else 0)
                        mm = p.emit("tensor", lambda t, ktl=ktl, ct=ct, us=us, w=w, P0=P0:
                                    t.matmul(P0[:, 0:w],
                                             wqk[ct % 2][:, ktl * 128:(ktl + 1) * 128],
                                             xb[:, ktl * TT_ + us: ktl * TT_ + us + w],
                                             start=(ktl == 0), stop=(ktl == NKT - 1)),
                                    waits=[(SQ[ct % 2], wqk_tok[ct]), ("sW", smalls)] +
                                          (P0_w if ktl == 0 else []),
                                    inc="sT")
                    bk.wrote(P0i, "sT", mm)
                    fin = qk_pipeline(ui % 2, P0, P0i, P1i, tabc, tabn, scl,
                                      tu, w, dst, dstoff, mm, tab_tok, half_guard)
                    pipe_done.append(fin)
                    ui += 1
                wqk_free[ct % 2] = p.cnt["sT"]
                if ct + 2 < 16:
                    load_wqk(ct + 2)
            vdone = []
            for i in range(10):
                for cb in range(2):
                    P0i = 4 + (i * 2 + cb) % 2
                    P0, P0_w = bk.acquire(P0i)
                    for ktl in range(NKT):
                        us = 2 + h * 1024 + i * 128 - (1 if ktl >= 4 else 0)
                        mm = p.emit("tensor", lambda t, ktl=ktl, us=us, cb=cb, P0=P0:
                                    t.matmul(P0[:],
                                             xb[:, ktl * TT_ + us: ktl * TT_ + us + 128],
                                             wv[:, ktl * D + cb * 512: ktl * D + cb * 512 + 512],
                                             start=(ktl == 0), stop=(ktl == NKT - 1)),
                                    waits=[("sW", smalls)] + (P0_w if ktl == 0 else []),
                                    inc="sT")
                    bk.wrote(P0i, "sT", mm)
                    vdst = vt.ap().rearrange("p (i g c) -> p i g c", i=10, c=65)[
                        :, i, cb * 8:(cb + 1) * 8, 0:64]
                    ev = p.emit("vector", lambda v, P0=P0, vdst=vdst: v.tensor_copy(
                        vdst, P0.ap().rearrange("p (g c) -> p g c", g=8)),
                        waits=bk.reader_waits(P0i) + [("sT", mm), ("sG", g_init),
                                                      ("sT", half_guard)], inc="sV")
                    bk.read(P0i, "sV", ev)
                    vdone.append(ev)
            qkv_done_v = p.cnt["sV"]

            tp_prev = None
            for s in range(8):
                S = h * 8 + s
                mA = maskA["maskA0"] if S == 0 else (maskA["maskA1"] if S == 1 else maskA["maskAr"])
                amax = []
                for hh in range(16):
                    pi = hh // 2
                    rb = 64 * (hh % 2)
                    Ai = hh % 2
                    A, A_w = bk.acquire(Ai)
                    am = p.emit("tensor", lambda t, pi=pi, rb=rb, s=s, A=A: t.matmul(
                        A[:, 0:384],
                        qt[rb:rb + 64, pi * HQ + s * 128: pi * HQ + s * 128 + 128],
                        kt_[rb:rb + 64, pi * HK + s * 128: pi * HK + s * 128 + 384],
                        start=True, stop=True),
                        waits=[("sV", qkv_done_v)] + A_w, inc="sT")
                    bk.wrote(Ai, "sT", am)
                    ma = p.emit("vector", lambda v, A=A, mA=mA, ui=hh % 2: v.tensor_add(
                        rsc[ui][:, 0:384], A[:, 0:384], mA[:]),
                        waits=bk.reader_waits(Ai) + [("sT", am), ("sW", smalls)], inc="sV")
                    bk.read(Ai, "sV", ma)
                    mx = p.emit("vector", lambda v, hh=hh, ui=hh % 2: v.reduce_max(
                        maxb[:, hh:hh + 1], rsc[ui][:, 0:384], axis=AX.X),
                        waits=[("sV", ma), ("sT", tp_prev)], inc="sV")
                    amax.append(mx)
                TMi = 2
                TM, TM_w = bk.acquire(TMi)
                tp = p.emit("tensor", lambda t, TM=TM: t.transpose(
                    TM[0:16, 0:128], maxb[:], idms[:]),
                    waits=[("sV", amax[-1]), ("sW", smalls)] + TM_w, inc="sT")
                tp_prev = tp
                bk.wrote(TMi, "sT", tp)
                ng = p.emit("scalar", lambda s_, TM=TM: s_.activation(
                    ngm[:], TM[0:16, 0:128], AF.Copy, scale=-1.0),
                    waits=bk.reader_waits(TMi) + [("sT", tp)], inc="sS")
                bk.read(TMi, "sS", ng)
                for hh in range(16):
                    pi = hh // 2
                    rb = 64 * (hh % 2)
                    Bi = 3 + hh % 2
                    B, B_w = bk.acquire(Bi)
                    for kt3 in range(3):
                        bm = p.emit("tensor", lambda t, pi=pi, rb=rb, s=s, kt3=kt3, B=B:
                                    t.matmul(B[:, kt3 * 128:(kt3 + 1) * 128],
                                             kt_[rb:rb + 64, pi * HK + s * 128 + kt3 * 128:
                                                 pi * HK + s * 128 + kt3 * 128 + 128],
                                             qt[rb:rb + 64, pi * HQ + s * 128:
                                                pi * HQ + s * 128 + 128],
                                             start=True, stop=False),
                                    waits=([("sS", ng)] + (B_w if kt3 == 0 else [])),
                                    inc="sT")
                        bm = p.emit("tensor", lambda t, hh=hh, kt3=kt3, B=B: t.matmul(
                            B[:, kt3 * 128:(kt3 + 1) * 128],
                            selms[:, hh * 128:(hh + 1) * 128], ngm[:],
                            start=False, stop=True),
                            inc="sT")
                    bk.wrote(Bi, "sT", bm)
                    if S == 0:
                        bmasks = [(0, "maskB00"), (1, "maskB01"), (2, "maskB02")]
                    elif S == 1:
                        bmasks = [(0, "maskB10"), (1, "maskB11"), (2, "maskB12")]
                    else:
                        bmasks = [(0, "maskBr0"), (2, "maskBr2")]
                    mt = bm
                    prev = bk.reader_waits(Bi)
                    for kt3, mn in bmasks:
                        mt = p.emit("vector", lambda v, kt3=kt3, mn=mn, B=B: v.tensor_add(
                            B[:, kt3 * 128:(kt3 + 1) * 128],
                            B[:, kt3 * 128:(kt3 + 1) * 128], maskB[mn][:]),
                            waits=[("sT", bm)] + prev, inc="sV")
                    bk.wrote(Bi, "sV", mt)
                    ex = p.emit("scalar", lambda s_, hh=hh, B=B: s_.activation(
                        att[hh % 2][:], B[:, 0:384], AF.Exp, scale=8.0),
                        waits=[("sV", mt), ("sT", bm), ("sT", att_free[hh % 2])],
                        inc="sS")
                    bk.read(Bi, "sS", ex)
                    AVi = 5 + hh % 2
                    AV, AV_w = bk.acquire(AVi)
                    for kt3 in range(3):
                        av = p.emit("tensor", lambda t, s=s, kt3=kt3, hh=hh, AV=AV: t.matmul(
                            AV[0:65, 0:128],
                            vt[:, (s + kt3) * 1040 + 65 * hh: (s + kt3) * 1040 + 65 * hh + 65],
                            att[hh % 2][:, kt3 * 128:(kt3 + 1) * 128],
                            start=(kt3 == 0), stop=(kt3 == 2)),
                            waits=[("sS", ex)] + (AV_w if kt3 == 0 else []), inc="sT")
                    bk.wrote(AVi, "sT", av)
                    att_free[hh % 2] = av
                    rc = p.emit("vector", lambda v, AV=AV: v.reciprocal(
                        rs1[:], AV[64:65, 0:128]),
                        waits=bk.reader_waits(AVi) + [("sT", av), ("sT", rs1_free[0])],
                        inc="sV")
                    BCi = 7
                    BC, BC_w = bk.acquire(BCi)
                    bc = p.emit("tensor", lambda t, BC=BC: t.matmul(
                        BC[0:64, 0:128], ones1[:], rs1[:], start=True, stop=True),
                        waits=[("sV", rc), ("sG", g_init)] + BC_w, inc="sT")
                    bk.wrote(BCi, "sT", bc)
                    rs1_free[0] = bc
                    ae = p.emit("scalar", lambda s_, hh=hh, AV=AV: s_.copy(
                        avs[hh % 2][:], AV[0:64, 0:128]),
                        waits=[("sT", av), ("sV", avs_free[hh % 2])], inc="sS")
                    bk.read(AVi, "sS", ae)
                    mxt = p.emit("vector", lambda v, hh=hh, pi=pi, rb=rb, s=s, BC=BC: v.tensor_mul(
                        mix[rb:rb + 64, pi * HQ + s * 128: pi * HQ + s * 128 + 128],
                        avs[hh % 2][:], BC[0:64, 0:128]),
                        waits=[("sS", ae), ("sT", bc), ("sT", half_guard)] +
                              bk.reader_waits(BCi), inc="sV")
                    bk.read(BCi, "sV", mxt)
                    avs_free[hh % 2] = mxt
            mix_done = p.cnt["sV"]

            for dt in range(NKT):
                for b in range(2):
                    Oi = (st_counter[0] % 4)
                    O, O_w = bk.acquire(Oi)
                    for j in range(NKT):
                        om = p.emit("tensor", lambda t, j=j, dt=dt, b=b, O=O: t.matmul(
                            O[:], wo[:, dt * D + j * 128: dt * D + (j + 1) * 128],
                            mix[:, j * HQ + b * 512: j * HQ + (b + 1) * 512],
                            start=(j == 0), stop=(j == NKT - 1)),
                            waits=[("sV", mix_done), ("sW", smalls)] +
                                  (O_w if j == 0 else []),
                            inc="sT")
                    bk.wrote(Oi, "sT", om)
                    sg = stage[st_counter[0] % 4]
                    et = p.emit("vector", lambda v, O=O, sg=sg: v.tensor_copy(sg[:], O[:]),
                                waits=bk.reader_waits(Oi) + [("sT", om),
                                                             (SST[st_counter[0] % 4], stage_last_read[st_counter[0] % 4])],
                                inc="sV")
                    bk.read(Oi, "sV", et)
                    ot = p.emit("gpsimd", lambda g, dt=dt, h=h, b=b, sg=sg: g.dma_start(
                        out=xout[dt * 128:(dt + 1) * 128,
                                 h * 1024 + b * 512: h * 1024 + (b + 1) * 512],
                        in_=sg[:], accum_op=ALU.add),
                        waits=[("sB", base_tok), ("sV", et)], inc=SST[st_counter[0] % 4], add=16)
                    stage_last_read[st_counter[0] % 4] = ot
                    st_counter[0] += 1
            half_guard = p.cnt["sT"]

        p.emit("sync", lambda e: e.nop(),
               waits=[("sB", base_tok)] + [(s, p.cnt[s]) for s in SST])

        with nc.Block() as block:
            p.run(block)

    return nc


SFULL = 8192


def build_l0attn():
    nc = bass.Bass("TRN2", target_bir_lowering=False, debug=False)
    xfull = nc.dram_tensor("xfull", [D, SFULL], F32, kind="ExternalInput").ap()
    wah = nc.dram_tensor("wah", [NKT, 128, 64], F32, kind="ExternalInput").ap()
    bcomb = nc.dram_tensor("bcomb", [64, 1], F32, kind="ExternalInput").ap()
    gci = nc.dram_tensor("gci", [128, NKT], F32, kind="ExternalInput").ap()
    blnci = nc.dram_tensor("blnci", [128, NKT], F32, kind="ExternalInput").ap()
    woh = nc.dram_tensor("woh", [NKT, 32, 128], F32, kind="ExternalInput").ap()
    boc = nc.dram_tensor("boc", [128, NKT], F32, kind="ExternalInput").ap()
    yout = nc.dram_tensor("yout", [D, SFULL], F32, kind="ExternalOutput").ap()

    NBF = SFULL // 512
    stack = ExitStack()
    with stack:
        E = stack.enter_context
        xblk = [E(nc.sbuf_tensor(f"xblk{i}", [128, NKT * 512], F32)) for i in range(2)]
        xnb = E(nc.sbuf_tensor("xnb", [128, NKT * 512], BF16))
        al_sc = E(nc.sbuf_tensor("al_sc", [32, SFULL], F32))
        b_sc = E(nc.sbuf_tensor("b_sc", [32, SFULL], F32))
        h_sc = E(nc.sbuf_tensor("h_sc", [32, SFULL], F32))
        hb = E(nc.sbuf_tensor("hb", [32, SFULL], BF16))
        wahs = E(nc.sbuf_tensor("wahs", [128, NKT * 64], BF16))
        wohs = E(nc.sbuf_tensor("wohs", [32, NKT * 128], BF16))
        bcs = E(nc.sbuf_tensor("bcs", [64, 1], F32))
        gsi = E(nc.sbuf_tensor("gsi", [128, NKT], F32))
        blsi = E(nc.sbuf_tensor("blsi", [128, NKT], F32))
        bos = E(nc.sbuf_tensor("bos", [128, NKT], F32))
        x2 = [E(nc.sbuf_tensor(f"x2_{i}", [128, 512], F32)) for i in range(2)]
        t1s = E(nc.sbuf_tensor("t1s", [128, 512], F32))
        t2s = E(nc.sbuf_tensor("t2s", [128, 512], F32))
        sfull = E(nc.sbuf_tensor("sfull", [128, NKT * 512], F32))
        ostg = [E(nc.sbuf_tensor(f"ostg{i}", [128, 512], F32)) for i in range(2)]
        rows = E(nc.sbuf_tensor("rows", [1, 6 * 512], F32))
        onesc = E(nc.sbuf_tensor("onesc", [128, 1], F32))
        ones1 = E(nc.sbuf_tensor("ones1", [1, 128], F32))
        cst = E(nc.sbuf_tensor("cst", [128, 1], F32))
        nc.const_aps.register(F32, LN_EPS, cst[:, 0:1])

        p = Prog(nc, stack)
        for s in ("sW", "sT", "sV", "sS", "sG", "sXa", "sXb", "sO0", "sO1"):
            p.sem(s)
        SX = ("sXa", "sXb")
        SO = ("sO0", "sO1")
        bk = Banks(nc, stack)

        p.emit("gpsimd", lambda g: g.memset(onesc[:], 1.0), inc="sG")
        p.emit("gpsimd", lambda g: g.memset(ones1[:], 1.0), inc="sG")
        p.emit("gpsimd", lambda g: g.memset(cst[:], LN_EPS), inc="sG")
        for src, dst in ((bcomb, bcs), (gci, gsi), (blnci, blsi), (boc, bos)):
            p.emit("gpsimd", lambda g, s=src, d=dst: g.dma_start(out=d[:], in_=s),
                   inc="sW", add=16)
        for ktl in range(NKT):
            p.emit("gpsimd", lambda g, ktl=ktl: g.dma_start(
                out=wahs[:, ktl * 64:(ktl + 1) * 64], in_=wah[ktl]), inc="sW", add=16)
            p.emit("gpsimd", lambda g, ktl=ktl: g.dma_start(
                out=wohs[:, ktl * 128:(ktl + 1) * 128], in_=woh[ktl]), inc="sW", add=16)
        smalls = p.cnt["sW"]

        xtok = [None] * NBF

        def load_xblk(i, waits=()):
            buf = xblk[i % 2]
            for ktl in range(NKT):
                tok = p.emit("sync", lambda e, ktl=ktl, i=i, buf=buf: e.dma_start(
                    out=buf[:, ktl * 512:(ktl + 1) * 512],
                    in_=xfull[ktl * 128:(ktl + 1) * 128, i * 512:(i + 1) * 512]),
                    waits=(waits if ktl == 0 else ()), inc=SX[i % 2], add=16)
            xtok[i] = (SX[i % 2], tok)

        load_xblk(0)
        load_xblk(1)

        def R(j):
            return rows[0:1, j * 512:(j + 1) * 512]

        def ln_block(buf, read_slice, xtk, rows_guard):
            bs, bs_w = bk.acquire(0)
            bq, bq_w = bk.acquire(1)
            for ktl in range(NKT):
                tok = p.emit("tensor", lambda t, ktl=ktl: t.matmul(
                    bs[0:1, :], onesc[:], read_slice(ktl),
                    start=(ktl == 0), stop=(ktl == NKT - 1)),
                    waits=[xtk, ("sG", 1)] + (bs_w if ktl == 0 else []), inc="sT")
            bk.wrote(0, "sT", tok)
            sq_toks = []
            for ktl in range(NKT):
                x2b = x2[ktl % 2]
                stok = p.emit("scalar", lambda s, ktl=ktl, x2b=x2b: s.activation(
                    x2b[:], read_slice(ktl), AF.Square),
                    waits=[xtk, ("sT", sq_toks[-2][1] if ktl >= 2 else None)], inc="sS")
                mtok = p.emit("tensor", lambda t, x2b=x2b, ktl=ktl: t.matmul(
                    bq[0:1, :], onesc[:], x2b[:],
                    start=(ktl == 0), stop=(ktl == NKT - 1)),
                    waits=[("sS", stok)] + (bq_w if ktl == 0 else []), inc="sT")
                sq_toks.append((stok, mtok))
            bk.wrote(1, "sT", mtok)
            t_m = p.emit("vector", lambda v: v.tensor_single_scalar(
                R(0), bs[0:1, :], 1.0 / D, ALU.mult),
                waits=[("sT", p.cnt["sT"]), ("sT", rows_guard)], inc="sV")
            bk.read(0, "sV", t_m)
            t_v1 = p.emit("vector", lambda v: v.tensor_single_scalar(
                R(1), bq[0:1, :], 1.0 / D, ALU.mult), inc="sV")
            bk.read(1, "sV", t_v1)
            t_m2 = p.emit("vector", lambda v: v.tensor_mul(R(2), R(0), R(0)),
                          waits=[("sV", t_m)], inc="sV")
            t_var = p.emit("vector", lambda v: v.tensor_sub(R(3), R(1), R(2)),
                           waits=[("sV", t_m2)], inc="sV")
            t_ln = p.emit("scalar", lambda s: s.activation(R(4), R(3), AF.Ln, bias=LN_EPS),
                          waits=[("sV", t_var), ("sG", 3)], inc="sS")
            t_rstd = p.emit("scalar", lambda s: s.activation(R(4), R(4), AF.Exp, scale=-0.5),
                            waits=[("sS", t_ln)], inc="sS")
            t_q = p.emit("vector", lambda v: v.scalar_tensor_tensor(
                R(5), R(0), -1.0, R(4), ALU.mult, ALU.mult),
                waits=[("sS", t_rstd)], inc="sV")
            bP, bP_w = bk.acquire(2)
            bQ, bQ_w = bk.acquire(3)
            t_bP = p.emit("tensor", lambda t: t.matmul(
                bP[:], ones1[:], R(4), start=True, stop=True),
                waits=[("sS", t_rstd), ("sV", t_q)] + bP_w, inc="sT")
            bk.wrote(2, "sT", t_bP)
            t_bQ = p.emit("tensor", lambda t: t.matmul(
                bQ[:], ones1[:], R(5), start=True, stop=True),
                waits=[("sV", t_q)] + bQ_w, inc="sT")
            bk.wrote(3, "sT", t_bQ)
            return bP, bQ, t_bQ

        rows_guard = None
        prev_apply = None
        proj_guard = None
        for i in range(NBF):
            buf = xblk[i % 2]
            rd = lambda ktl, buf=buf: buf[:, ktl * 512:(ktl + 1) * 512]
            bP, bQ, t_bQ = ln_block(buf, rd, xtok[i], rows_guard)
            rows_guard = t_bQ
            prev_tt2 = None
            last_x = None
            for ktl in range(NKT):
                tt1 = p.emit("vector", lambda v, ktl=ktl, buf=buf, bP=bP: v.tensor_mul(
                    t1s[:], buf[:, ktl * 512:(ktl + 1) * 512], bP[:]),
                    waits=[("sT", t_bQ), ("sV", prev_tt2)], inc="sV")
                tt2 = p.emit("vector", lambda v, bQ=bQ: v.tensor_add(t2s[:], t1s[:], bQ[:]),
                             waits=[("sV", tt1), ("sV", prev_apply)], inc="sV")
                prev_tt2 = tt2
                last_x = p.emit("vector", lambda v, ktl=ktl: v.tensor_scalar(
                    xnb[:, ktl * 512:(ktl + 1) * 512], t2s[:],
                    gsi[:, ktl:ktl + 1], blsi[:, ktl:ktl + 1], ALU.mult, ALU.add),
                    waits=[("sV", tt2), ("sW", smalls), ("sT", proj_guard)], inc="sV")
                prev_apply = last_x
            bk.read(2, "sV", last_x)
            bk.read(3, "sV", last_x)
            pj, pj_w = bk.acquire(4 + i % 2)
            for ktl in range(NKT):
                pm = p.emit("tensor", lambda t, ktl=ktl, pj=pj: t.matmul(
                    pj[0:64, :], wahs[:, ktl * 64:(ktl + 1) * 64],
                    xnb[:, ktl * 512:(ktl + 1) * 512],
                    start=(ktl == 0), stop=(ktl == NKT - 1)),
                    waits=[("sV", last_x), ("sW", smalls)] + (pj_w if ktl == 0 else []),
                    inc="sT")
            bk.wrote(4 + i % 2, "sT", pm)
            proj_guard = pm
            a1 = p.emit("scalar", lambda s, pj=pj, i=i: s.activation(
                al_sc[:, i * 512:(i + 1) * 512], pj[0:32, :], AF.Sigmoid,
                bias=bcs[0:32, 0:1]),
                waits=bk.reader_waits(4 + i % 2) + [("sT", pm)], inc="sS")
            bk.read(4 + i % 2, "sS", a1)
            a2 = p.emit("vector", lambda v, pj=pj, i=i: v.tensor_scalar_add(
                b_sc[:, i * 512:(i + 1) * 512], pj[32:64, :], bcs[32:64, 0:1]),
                waits=[("sT", pm)], inc="sV")
            bk.read(4 + i % 2, "sV", a2)
            if i + 2 < NBF:
                load_xblk(i + 2, waits=[("sV", last_x), ("sS", p.cnt["sS"]),
                                        ("sT", p.cnt["sT"])])

        sc = p.emit("vector", lambda v: v.tensor_tensor_scan(
            h_sc[:], al_sc[:], b_sc[:], 0.0, ALU.mult, ALU.add),
            waits=[("sS", p.cnt["sS"]), ("sV", p.cnt["sV"])], inc="sV")
        hc = p.emit("vector", lambda v: v.tensor_copy(hb[:], h_sc[:]),
                    waits=[("sV", sc)], inc="sV")

        stg_last = [None, None]
        for i in range(NBF):
            buf = xblk[i % 2]
            load_xblk(i, waits=[("sV", p.cnt["sV"]), ("sS", p.cnt["sS"]),
                                ("sT", p.cnt["sT"])] if i < 2 else
                      [("sV", p.cnt["sV"])])
            sf_done = None
            for dt in range(NKT):
                pb, pb_w = bk.acquire(4 + dt % 4)
                pm = p.emit("tensor", lambda t, dt=dt, i=i, pb=pb: t.matmul(
                    pb[:], wohs[:, dt * 128:(dt + 1) * 128],
                    hb[:, i * 512:(i + 1) * 512],
                    start=True, stop=True),
                    waits=[("sV", hc), ("sW", smalls)] + pb_w,
                    inc="sT")
                bk.wrote(4 + dt % 4, "sT", pm)
                ev = p.emit("vector", lambda v, dt=dt, pb=pb, buf=buf: v.scalar_tensor_tensor(
                    sfull[:, dt * 512:(dt + 1) * 512], pb[:], bos[:, dt:dt + 1],
                    buf[:, dt * 512:(dt + 1) * 512], ALU.add, ALU.add),
                    waits=bk.reader_waits(4 + dt % 4) + [("sT", pm), xtok[i]],
                    inc="sV")
                bk.read(4 + dt % 4, "sV", ev)
                ev = p.emit("vector", lambda v, dt=dt, buf=buf: v.tensor_add(
                    sfull[:, dt * 512:(dt + 1) * 512],
                    sfull[:, dt * 512:(dt + 1) * 512],
                    buf[:, dt * 512:(dt + 1) * 512]),
                    waits=[("sV", ev)], inc="sV")
                sf_done = ev
            rd = lambda ktl: sfull[:, ktl * 512:(ktl + 1) * 512]
            bP, bQ, t_bQ = ln_block(None, rd, ("sV", sf_done), rows_guard)
            rows_guard = t_bQ
            prev_tt2 = None
            last_o = None
            for ktl in range(NKT):
                tt1 = p.emit("vector", lambda v, ktl=ktl, bP=bP: v.tensor_mul(
                    t1s[:], sfull[:, ktl * 512:(ktl + 1) * 512], bP[:]),
                    waits=[("sT", t_bQ), ("sV", prev_tt2)], inc="sV")
                og = ostg[ktl % 2]
                tt2 = p.emit("vector", lambda v, bQ=bQ, og=og, ktl=ktl: v.tensor_add(
                    og[:], t1s[:], bQ[:]),
                    waits=[("sV", tt1), (SO[ktl % 2], stg_last[ktl % 2])], inc="sV")
                prev_tt2 = tt2
                last_o = tt2
                ot = p.emit("sync", lambda e, ktl=ktl, i=i, og=og: e.dma_start(
                    out=yout[ktl * 128:(ktl + 1) * 128, i * 512:(i + 1) * 512],
                    in_=og[:]),
                    waits=[("sV", tt2)], inc=SO[ktl % 2], add=16)
                stg_last[ktl % 2] = ot
            bk.read(2, "sV", last_o)
            bk.read(3, "sV", last_o)

        p.emit("sync", lambda e: e.nop(),
               waits=[(s, p.cnt[s]) for s in SO])

        with nc.Block() as block:
            p.run(block)

    return nc


def build_l1():
    nc = bass.Bass("TRN2", target_bir_lowering=False, debug=False)
    xfull = nc.dram_tensor("xfull", [D, SFULL], F32, kind="ExternalInput").ap()
    ac = nc.dram_tensor("ac", [128, NKT], F32, kind="ExternalInput").ap()
    bc_ = nc.dram_tensor("bc_", [128, NKT], F32, kind="ExternalInput").ap()
    w1h_ = nc.dram_tensor("w1h_", [NKT, 128, D], F32, kind="ExternalInput").ap()
    boc = nc.dram_tensor("boc", [128, NKT], F32, kind="ExternalInput").ap()
    yout = nc.dram_tensor("yout", [D, SFULL], F32, kind="ExternalOutput").ap()
    ybuf = nc.dram_tensor("ybuf", [D, SFULL], BF16, kind="Internal").ap()

    NBF = SFULL // 512
    stack = ExitStack()
    with stack:
        E = stack.enter_context
        xkt = E(nc.sbuf_tensor("xkt", [128, SFULL], F32))
        bx = E(nc.sbuf_tensor("bx", [128, SFULL], F32))
        ybk = E(nc.sbuf_tensor("ybk", [128, SFULL], BF16))
        yblk = [E(nc.sbuf_tensor(f"yblk{i}", [128, NKT * 512], BF16)) for i in range(2)]
        xblk = [E(nc.sbuf_tensor(f"xblk{i}", [128, NKT * 512], F32)) for i in range(2)]
        wts = E(nc.sbuf_tensor("wts", [128, NKT * D], BF16))
        acs = E(nc.sbuf_tensor("acs", [128, NKT], F32))
        ahs = E(nc.sbuf_tensor("ahs", [128, NKT], F32))
        bcs = E(nc.sbuf_tensor("bcs", [128, NKT], F32))
        bos = E(nc.sbuf_tensor("bos", [128, NKT], F32))
        sfull = E(nc.sbuf_tensor("sfull", [128, NKT * 512], F32))
        ostg = [E(nc.sbuf_tensor(f"ostg{i}", [128, 512], F32)) for i in range(2)]
        x2 = [E(nc.sbuf_tensor(f"x2_{i}", [128, 512], F32)) for i in range(2)]
        t1s = E(nc.sbuf_tensor("t1s", [128, 512], F32))
        rows = E(nc.sbuf_tensor("rows", [1, 6 * 512], F32))
        onesc = E(nc.sbuf_tensor("onesc", [128, 1], F32))
        ones1 = E(nc.sbuf_tensor("ones1", [1, 128], F32))
        cst = E(nc.sbuf_tensor("cst", [128, 1], F32))
        nc.const_aps.register(F32, LN_EPS, cst[:, 0:1])

        p = Prog(nc, stack)
        for s in ("sW", "sT", "sV", "sS", "sG", "sXa", "sXb",
                  "sYa", "sYb", "sY", "sO0", "sO1", "sK"):
            p.sem(s)
        SX = ("sXa", "sXb")
        SY = ("sYa", "sYb")
        SO = ("sO0", "sO1")
        bk = Banks(nc, stack)

        p.emit("gpsimd", lambda g: g.memset(onesc[:], 1.0), inc="sG")
        p.emit("gpsimd", lambda g: g.memset(ones1[:], 1.0), inc="sG")
        p.emit("gpsimd", lambda g: g.memset(cst[:], LN_EPS), inc="sG")
        for src, dst in ((ac, acs), (bc_, bcs), (boc, bos)):
            p.emit("gpsimd", lambda g, s=src, d=dst: g.dma_start(out=d[:], in_=s),
                   inc="sW", add=16)
        for ktl in range(NKT):
            p.emit("gpsimd", lambda g, ktl=ktl: g.dma_start(
                out=wts[:, ktl * D:(ktl + 1) * D], in_=w1h_[ktl]), inc="sW", add=16)
        smalls = p.cnt["sW"]
        th = p.emit("scalar", lambda s: s.activation(ahs[:], acs[:], AF.Tanh),
                    waits=[("sW", smalls)], inc="sS")

        prev_scan = None
        prev_store = None
        for ktl in range(NKT):
            xl = p.emit("sync", lambda e, ktl=ktl: e.dma_start(
                out=xkt[:], in_=xfull[ktl * 128:(ktl + 1) * 128, :]),
                waits=[("sV", prev_scan)], inc="sK", add=16)
            bxt = p.emit("vector", lambda v, ktl=ktl: v.tensor_single_scalar(
                bx[:], xkt[:], bcs[:, ktl:ktl + 1], ALU.mult),
                waits=[("sK", xl), ("sW", smalls), ("sV", prev_scan)], inc="sV")
            ab = p.emit("vector", lambda v, ktl=ktl: v.tensor_scalar(
                xkt[:], xkt[:], 0.0, ahs[:, ktl:ktl + 1], ALU.mult, ALU.add),
                waits=[("sV", bxt), ("sS", th)], inc="sV")
            scn = p.emit("vector", lambda v: v.tensor_tensor_scan(
                ybk[:], xkt[:], bx[:], 0.0, ALU.mult, ALU.add),
                waits=[("sV", ab), ("sY", prev_store)], inc="sV")
            prev_scan = scn
            prev_store = p.emit("sync", lambda e, ktl=ktl: e.dma_start(
                out=ybuf[ktl * 128:(ktl + 1) * 128, :], in_=ybk[:]),
                waits=[("sV", scn)], inc="sY", add=16)

        def R(j):
            return rows[0:1, j * 512:(j + 1) * 512]

        xtok = [None] * NBF
        ytok = [None] * NBF

        def load_blk(i, waits=()):
            bufx, bufy = xblk[i % 2], yblk[i % 2]
            for ktl in range(NKT):
                p.emit("sync", lambda e, ktl=ktl, i=i, bufx=bufx: e.dma_start(
                    out=bufx[:, ktl * 512:(ktl + 1) * 512],
                    in_=xfull[ktl * 128:(ktl + 1) * 128, i * 512:(i + 1) * 512]),
                    waits=(waits if ktl == 0 else ()), inc=SX[i % 2], add=16)
                ytok[i] = p.emit("sync", lambda e, ktl=ktl, i=i, bufy=bufy: e.dma_start(
                    out=bufy[:, ktl * 512:(ktl + 1) * 512],
                    in_=ybuf[ktl * 128:(ktl + 1) * 128, i * 512:(i + 1) * 512]),
                    inc=SY[i % 2], add=16)
            xtok[i] = (SX[i % 2], p.cnt[SX[i % 2]])
            ytok[i] = (SY[i % 2], ytok[i])

        load_blk(0, waits=[("sY", prev_store)])
        load_blk(1)

        rows_guard = None
        stg_last = [None, None]
        for i in range(NBF):
            bufx, bufy = xblk[i % 2], yblk[i % 2]
            sf_done = None
            for dt in range(NKT):
                pb, pb_w = bk.acquire(4 + dt % 4)
                for ktl in range(NKT):
                    pm = p.emit("tensor", lambda t, dt=dt, ktl=ktl, pb=pb, bufy=bufy:
                                t.matmul(pb[:],
                                         wts[:, dt * D + ktl * 128: dt * D + (ktl + 1) * 128],
                                         bufy[:, ktl * 512:(ktl + 1) * 512],
                                         start=(ktl == 0), stop=(ktl == NKT - 1)),
                                waits=[ytok[i], ("sW", smalls)] +
                                      (pb_w if ktl == 0 else []),
                                inc="sT")
                bk.wrote(4 + dt % 4, "sT", pm)
                ev = p.emit("vector", lambda v, dt=dt, pb=pb, bufx=bufx: v.scalar_tensor_tensor(
                    sfull[:, dt * 512:(dt + 1) * 512], pb[:], bos[:, dt:dt + 1],
                    bufx[:, dt * 512:(dt + 1) * 512], ALU.add, ALU.add),
                    waits=bk.reader_waits(4 + dt % 4) + [("sT", pm), xtok[i],
                                                         ("sV", sf_done)],
                    inc="sV")
                bk.read(4 + dt % 4, "sV", ev)
                sf_done = ev
            bs, bs_w = bk.acquire(0)
            bq, bq_w = bk.acquire(1)
            for ktl in range(NKT):
                tok = p.emit("tensor", lambda t, ktl=ktl: t.matmul(
                    bs[0:1, :], onesc[:], sfull[:, ktl * 512:(ktl + 1) * 512],
                    start=(ktl == 0), stop=(ktl == NKT - 1)),
                    waits=[("sV", sf_done), ("sG", 1)] + (bs_w if ktl == 0 else []),
                    inc="sT")
            bk.wrote(0, "sT", tok)
            sq_toks = []
            for ktl in range(NKT):
                x2b = x2[ktl % 2]
                stok = p.emit("scalar", lambda s, ktl=ktl, x2b=x2b: s.activation(
                    x2b[:], sfull[:, ktl * 512:(ktl + 1) * 512], AF.Square),
                    waits=[("sV", sf_done),
                           ("sT", sq_toks[-2][1] if ktl >= 2 else None)], inc="sS")
                mtok = p.emit("tensor", lambda t, x2b=x2b, ktl=ktl, bq=bq: t.matmul(
                    bq[0:1, :], onesc[:], x2b[:],
                    start=(ktl == 0), stop=(ktl == NKT - 1)),
                    waits=[("sS", stok)] + (bq_w if ktl == 0 else []), inc="sT")
                sq_toks.append((stok, mtok))
            bk.wrote(1, "sT", mtok)
            t_m = p.emit("vector", lambda v: v.tensor_single_scalar(
                R(0), bs[0:1, :], 1.0 / D, ALU.mult),
                waits=[("sT", p.cnt["sT"]), ("sT", rows_guard)], inc="sV")
            bk.read(0, "sV", t_m)
            t_v1 = p.emit("vector", lambda v: v.tensor_single_scalar(
                R(1), bq[0:1, :], 1.0 / D, ALU.mult), inc="sV")
            bk.read(1, "sV", t_v1)
            t_m2 = p.emit("vector", lambda v: v.tensor_mul(R(2), R(0), R(0)),
                          waits=[("sV", t_m)], inc="sV")
            t_var = p.emit("vector", lambda v: v.tensor_sub(R(3), R(1), R(2)),
                           waits=[("sV", t_m2)], inc="sV")
            t_ln = p.emit("scalar", lambda s: s.activation(R(4), R(3), AF.Ln, bias=LN_EPS),
                          waits=[("sV", t_var), ("sG", 3)], inc="sS")
            t_rstd = p.emit("scalar", lambda s: s.activation(R(4), R(4), AF.Exp, scale=-0.5),
                            waits=[("sS", t_ln)], inc="sS")
            t_q = p.emit("vector", lambda v: v.scalar_tensor_tensor(
                R(5), R(0), -1.0, R(4), ALU.mult, ALU.mult),
                waits=[("sS", t_rstd)], inc="sV")
            bP, bP_w = bk.acquire(2)
            bQ, bQ_w = bk.acquire(3)
            t_bP = p.emit("tensor", lambda t, bP=bP: t.matmul(
                bP[:], ones1[:], R(4), start=True, stop=True),
                waits=[("sS", t_rstd), ("sV", t_q)] + bP_w, inc="sT")
            bk.wrote(2, "sT", t_bP)
            t_bQ = p.emit("tensor", lambda t, bQ=bQ: t.matmul(
                bQ[:], ones1[:], R(5), start=True, stop=True),
                waits=[("sV", t_q)] + bQ_w, inc="sT")
            bk.wrote(3, "sT", t_bQ)
            rows_guard = t_bQ
            prev_tt2 = None
            last_o = None
            for ktl in range(NKT):
                tt1 = p.emit("vector", lambda v, ktl=ktl, bP=bP: v.tensor_mul(
                    t1s[:], sfull[:, ktl * 512:(ktl + 1) * 512], bP[:]),
                    waits=[("sT", t_bQ), ("sV", prev_tt2)], inc="sV")
                og = ostg[ktl % 2]
                tt2 = p.emit("vector", lambda v, bQ=bQ, og=og: v.tensor_add(
                    og[:], t1s[:], bQ[:]),
                    waits=[("sV", tt1), (SO[ktl % 2], stg_last[ktl % 2])], inc="sV")
                prev_tt2 = tt2
                last_o = tt2
                ot = p.emit("sync", lambda e, ktl=ktl, i=i, og=og: e.dma_start(
                    out=yout[ktl * 128:(ktl + 1) * 128, i * 512:(i + 1) * 512],
                    in_=og[:]),
                    waits=[("sV", tt2)], inc=SO[ktl % 2], add=16)
                stg_last[ktl % 2] = ot
            bk.read(2, "sV", last_o)
            bk.read(3, "sV", last_o)
            if i + 2 < NBF:
                load_blk(i + 2, waits=[("sV", last_o), ("sS", p.cnt["sS"]),
                                       ("sT", p.cnt["sT"])])

        p.emit("sync", lambda e: e.nop(), waits=[(s, p.cnt[s]) for s in SO])

        with nc.Block() as block:
            p.run(block)

    return nc


def prep_l0attn_weights(W_a, b_a, W_in, b_in, W_out, b_out, ln_g, ln_b):
    Wc = np.concatenate([W_a, W_in], 0)
    wah = np.ascontiguousarray(Wc.T.reshape(NKT, 128, 64))
    bcomb = np.concatenate([b_a, b_in])[:, None].astype(np.float32)
    woh = np.ascontiguousarray(
        W_out.reshape(NKT, 128, 32).transpose(0, 2, 1))
    return dict(wah=wah, bcomb=np.ascontiguousarray(bcomb),
                gci=np.ascontiguousarray(ln_g.reshape(NKT, 128).T),
                blnci=np.ascontiguousarray(ln_b.reshape(NKT, 128).T),
                woh=woh,
                boc=np.ascontiguousarray(b_out.reshape(NKT, 128).T))


def prep_l1_weights(a, b, w_out, b_out):
    w1h_ = np.ascontiguousarray(
        w_out.T.reshape(NKT, 128, NKT, 128).transpose(2, 1, 0, 3).reshape(NKT, 128, D))
    return dict(ac=np.ascontiguousarray(a.reshape(NKT, 128).T),
                bc_=np.ascontiguousarray(b.reshape(NKT, 128).T),
                w1h_=w1h_,
                boc=np.ascontiguousarray(b_out.reshape(NKT, 128).T))


def make_rotary_tables():
    n = 8192
    DH = 64
    W = 256
    half = np.arange(0, DH, 2, dtype=np.float32)
    inv_freq = (1.0 / 10000.0 ** (half / DH)).astype(np.float32)
    t = np.arange(n, dtype=np.float32)
    freqs = (t[:, None] * inv_freq[None, :]).astype(np.float32)
    freqs = np.concatenate([freqs, freqs], -1)
    base = ((half + 0.4 * DH) / (1.4 * DH)).astype(np.float32)
    power = ((t - n // 2) / (W // 2)).astype(np.float32)
    scale = (base[None, :].astype(np.float64) ** power[:, None].astype(np.float64))
    scale = np.concatenate([scale, scale], -1)
    cos = np.cos(freqs.astype(np.float64))
    sin = np.sin(freqs.astype(np.float64))
    sgn = np.where(np.arange(DH) < 32, -1.0, 1.0)[None, :]
    qc = (cos * scale).astype(np.float32)
    qs = (sgn * sin * scale).astype(np.float32)
    kc = (cos / scale).astype(np.float32)
    ks = (sgn * sin / scale).astype(np.float32)
    return qc, qs, kc, ks


_ROT = None


def attn_tables_for_core(c):
    global _ROT
    if _ROT is None:
        _ROT = make_rotary_tables()
    q = c % 4
    toks = np.clip(np.arange(q * T - HALO, q * T + T), 0, 8191)
    out = {}
    for name, tab in zip(("tqc", "tqs", "tkc", "tks"), _ROT):
        m = tab[toks].T
        out[name] = np.ascontiguousarray(np.concatenate([m, m], 0))
    return out


def _allowed(gi, gj):
    return (gj >= 0) & (gi - gj >= 0) & (gi - gj <= 256)


def attn_masks_for_core(c):
    q = c % 4
    tq0 = q * T
    out = {}
    r = np.arange(128)
    for name, S in (("maskA0", 0), ("maskA1", 1), ("maskAr", 2)):
        gi = tq0 + S * 128 + r[:, None]
        gj = tq0 + S * 128 - 256 + np.arange(384)[None, :]
        out[name] = np.where(_allowed(gi, gj), 0.0, NEG).astype(np.float32)
    for name, S, kt3 in (("maskB00", 0, 0), ("maskB01", 0, 1), ("maskB02", 0, 2),
                         ("maskB10", 1, 0), ("maskB11", 1, 1), ("maskB12", 1, 2),
                         ("maskBr0", 2, 0), ("maskBr2", 2, 2)):
        gj = tq0 + S * 128 - 256 + kt3 * 128 + r[:, None]
        gi = tq0 + S * 128 + np.arange(128)[None, :]
        out[name] = np.where(_allowed(gi, gj), 0.0, NEG).astype(np.float32)
    return out


def prep_attn_weights(Wqkv, Wo):
    wqkh = np.ascontiguousarray(
        Wqkv[:, :2048].reshape(NKT, 128, 16, 128).transpose(2, 1, 0, 3).reshape(16, 128, D))
    wvh = np.ascontiguousarray(Wqkv[:, 2048:].reshape(NKT, 128, D))
    woh = np.ascontiguousarray(
        Wo.reshape(NKT, 128, NKT, 128).transpose(2, 1, 0, 3).reshape(NKT, 128, D))
    return dict(wqkh=wqkh, wvh=wvh, woh=woh)


def attn_const_inputs():
    idm = np.eye(128, dtype=np.float32)
    selm = np.zeros((16, 16 * 128), np.float32)
    for hh in range(16):
        selm[hh, hh * 128:(hh + 1) * 128] = 1.0
    return dict(idm=idm, selm=selm)


def shard_xT_attn(xfull):
    outs = []
    for c in range(8):
        b, q = c // 4, c % 4
        lo = q * T - HALO
        xin = np.zeros((D, T + HALO), np.float32)
        src = xfull[b, max(lo, 0):q * T + T, :].T
        xin[:, (T + HALO) - src.shape[1]:] = src
        outs.append(np.ascontiguousarray(xin))
    return outs



def prep_ffn_weights(W1, b1, W2, b2, g, bln):
    W1p = np.zeros((D, 2 * DIP), np.float32)
    W1p[:, :DI] = W1[:, :DI]
    W1p[:, DIP:DIP + DI] = W1[:, DI:]
    b1p = np.zeros(2 * DIP, np.float32)
    b1p[:DI] = b1[:DI]
    b1p[DIP:DIP + DI] = b1[DI:]
    W2p = np.zeros((DIP, D), np.float32)
    W2p[:DI] = W2
    w1h = np.ascontiguousarray(
        W1p.reshape(NKT, 128, 2 * NJ, 128).transpose(2, 1, 0, 3).reshape(2 * NJ, 128, D))
    w2h = np.ascontiguousarray(
        W2p.reshape(NJ, 128, NKT, 128).transpose(2, 1, 0, 3).reshape(NKT, 128, DIP))
    b1c = np.ascontiguousarray(b1p.reshape(2 * NJ, 128).T)
    b2c = np.ascontiguousarray(b2.reshape(NKT, 128).T)
    gcc = np.ascontiguousarray(g.reshape(NKT, 128).T)
    blncc = np.ascontiguousarray(bln.reshape(NKT, 128).T)
    return dict(w1h=w1h, w2h=w2h, b1c=b1c, b2c=b2c, gc=gcc, blnc=blncc)


def shard_xT(xfull):
    outs = []
    for c in range(8):
        b, q = c // 4, c % 4
        lo = q * T - 1
        xin = np.zeros((D, T + 1), np.float32)
        src = xfull[b, max(lo, 0):q * T + T, :].T
        xin[:, (T + 1) - src.shape[1]:] = src
        outs.append(np.ascontiguousarray(xin))
    return outs


def gather_chunks(res, key="xout"):
    out = np.empty((2, 8192, D), np.float32)
    for c in range(8):
        b, q = c // 4, c % 4
        out[b, q * T:(q + 1) * T, :] = res[c][key].T
    return out


_PROGS = {}
_CORE_CONSTS = None


def _progs():
    if not _PROGS:
        _PROGS["attn"] = build_attn()
        _PROGS["ffn"] = build_ffn(False)
        _PROGS["ffn_last"] = build_ffn(True)
        _PROGS["l0"] = build_l0attn()
        _PROGS["l1"] = build_l1()
    return _PROGS


def _core_consts():
    global _CORE_CONSTS
    if _CORE_CONSTS is None:
        cc = attn_const_inputs()
        _CORE_CONSTS = []
        for c in range(8):
            d = dict(cc)
            d.update(attn_tables_for_core(c))
            d.update(attn_masks_for_core(c))
            _CORE_CONSTS.append(d)
    return _CORE_CONSTS


def _run(nc, in_maps):
    return run_bass_kernel_spmd(nc, in_maps, core_ids=list(range(8))).results


def _ffn_slot(P, X, Wqkv, q_scale, k_scale, Wo, ln_g, ln_b, W1, b1, W2, b2):
    consts = _core_consts()
    X0 = X
    for i in range(6):
        aw = prep_attn_weights(np.ascontiguousarray(Wqkv[i]),
                               np.ascontiguousarray(Wo[i]))
        qsc = np.ascontiguousarray(np.concatenate([q_scale[i], q_scale[i]])[:, None])
        ksc = np.ascontiguousarray(np.concatenate([k_scale[i], k_scale[i]])[:, None])
        xins = shard_xT_attn(X)
        im = [dict(xin=xins[c], qsc=qsc, ksc=ksc, **aw, **consts[c])
              for c in range(8)]
        X = gather_chunks(_run(P["attn"], im))
        fw = prep_ffn_weights(np.ascontiguousarray(W1[i]), np.ascontiguousarray(b1[i]),
                              np.ascontiguousarray(W2[i]), np.ascontiguousarray(b2[i]),
                              np.ascontiguousarray(ln_g[i]), np.ascontiguousarray(ln_b[i]))
        xins = shard_xT(X)
        if i == 5:
            im = [dict(xin=xins[c],
                       xslot=np.ascontiguousarray(X0[c // 4, (c % 4) * T:(c % 4 + 1) * T].T),
                       **fw) for c in range(8)]
            X = gather_chunks(_run(P["ffn_last"], im))
        else:
            im = [dict(xin=xins[c], **fw) for c in range(8)]
            X = gather_chunks(_run(P["ffn"], im))
    return X


def kernel(**inputs):
    f = {k: np.ascontiguousarray(np.asarray(v, dtype=np.float32))
         for k, v in inputs.items()}
    P = _progs()

    w0 = prep_l0attn_weights(f["L0_attn_W_a"], f["L0_attn_b_a"], f["L0_attn_W_in"],
                             f["L0_attn_b_in"], f["L0_attn_W_out"], f["L0_attn_b_out"],
                             f["L0_attn_ln_gamma"], f["L0_attn_ln_beta"])
    x = f["L0_attn_x"]
    im = [dict(xfull=np.ascontiguousarray(x[c // 4].T), **w0) for c in range(8)]
    res = _run(P["l0"], im)
    X = np.stack([res[0]["yout"].T, res[4]["yout"].T])

    X = _ffn_slot(P, X, f["L0_ffn_Wqkv"], f["L0_ffn_q_scale"], f["L0_ffn_k_scale"],
                  f["L0_ffn_Wo"], f["L0_ffn_ln_g"], f["L0_ffn_ln_b"],
                  f["L0_ffn_W1"], f["L0_ffn_b1"], f["L0_ffn_W2"], f["L0_ffn_b2"])

    w1 = prep_l1_weights(f["L1_attn_a"], f["L1_attn_b"], f["L1_attn_w_out"],
                         f["L1_attn_b_out"])
    im = [dict(xfull=np.ascontiguousarray(X[c // 4].T), **w1) for c in range(8)]
    res = _run(P["l1"], im)
    X = np.stack([res[0]["yout"].T, res[4]["yout"].T])

    X = _ffn_slot(P, X, f["L1_ffn_Wqkv"], f["L1_ffn_q_scale"], f["L1_ffn_k_scale"],
                  f["L1_ffn_Wo"], f["L1_ffn_ln_g"], f["L1_ffn_ln_b"],
                  f["L1_ffn_W1"], f["L1_ffn_b1"], f["L1_ffn_W2"], f["L1_ffn_b2"])

    return np.ascontiguousarray(X)
